# version 02
# baseline (speedup 1.0000x reference)
import math
from contextlib import ExitStack

import numpy as np

import concourse.bass as bass
import concourse.mybir as mybir
from concourse.bass_utils import run_bass_kernel_spmd

F32 = mybir.dt.float32
AF = mybir.ActivationFunctionType
ALU = mybir.AluOpType
AX = mybir.AxisListType

B, T, D = 2, 4096, 2048
H, DK = 16, 128
EXP, IFF = 16, 1408
NE = 18
NCORE = 8
TPC = 1024
NT = B * T
SCALE = 1.0 / math.sqrt(DK)
EPS = 1e-5
NEGINF = -1.0e30
NDC = D // 128
NIC = IFF // 128

ENGS = ("sync", "gpsimd", "scalar", "vector", "tensor")


class Res:

    def __init__(self, prog):
        self.p = prog
        self.w = {}
        self.r = {}

    def read(self, eng):
        for s, v in self.w.items():
            self.p.wait(eng, s, v)

    def write(self, eng):
        for s, v in self.w.items():
            self.p.wait(eng, s, v)
        for s, v in self.r.items():
            self.p.wait(eng, s, v)

    def setw(self, sem, val):
        if val is not None:
            self.w[sem] = max(self.w.get(sem, 0), val)

    def setr(self, sem, val):
        if val is not None:
            self.r[sem] = max(self.r.get(sem, 0), val)


class Prog:
    def __init__(self, nc, sems):
        self.nc = nc
        self.sems = sems
        self.cnt = {k: 0 for k in sems}
        self.waited = {e: {} for e in ENGS}
        self.fns = {e: [] for e in ENGS}

    def wait(self, eng, sem, val, force=False):
        if val <= 0:
            return
        if not force and self.waited[eng].get(sem, 0) >= val:
            return
        self.waited[eng][sem] = max(self.waited[eng].get(sem, 0), val)
        h = self.sems[sem]
        self.fns[eng].append(lambda e, h=h, v=val: e.wait_ge(h, v))

    def op(self, eng, make, sem=None, amt=1):
        if sem is None:
            self.fns[eng].append(make)
            return None
        self.cnt[sem] += amt
        tgt = self.cnt[sem]
        h = self.sems[sem]
        self.fns[eng].append(lambda e, make=make, h=h, a=amt: make(e).then_inc(h, a))
        return tgt

    def stream(self, name):
        if name not in self.sems:
            self.sems[name] = self.nc.alloc_semaphore(name=name)
            self.cnt[name] = 0
        return name

    def dma(self, eng, stream, out, in_, reads=(), writes=()):
        sem = self.stream(stream)
        for r in reads:
            r.read(eng)
        for w in writes:
            w.write(eng)
        v = self.op(eng, lambda e, o=out, i=in_: e.dma_start(out=o, in_=i), sem, 16)
        for r in reads:
            r.setr(sem, v)
        for w in writes:
            w.setw(sem, v)
        return sem, v

    def join_all(self):
        snap = dict(self.cnt)
        for e in ENGS:
            for s, v in snap.items():
                self.wait(e, s, v)


def ceil_div(a, b):
    return (a + b - 1) // b


def build_nc():
    nc = bass.Bass("TRN2", target_bir_lowering=False)
    st = ExitStack()

    def din(name, shape):
        return nc.dram_tensor(name, list(shape), F32, kind="ExternalInput").ap()

    xfm = din("xfm", (D, NT))
    x_own = din("x_own", (TPC, D))
    qkvw = din("qkvw", (D, 768))
    wproj = din("wproj", (256, D))
    gatew = din("gatew", (D, EXP))
    w13 = din("w13", (NE, D, NIC, 256))
    w2 = din("w2", (NE, IFF, D))
    iden_in = din("iden", (128, 128))
    masks_in = din("masks", (128, 4, 512))

    out = nc.dram_tensor("out", [TPC, D], F32, kind="ExternalOutput").ap()

    q_scr = nc.dram_tensor("q_scr", [2, 2, 128, T], F32).ap()
    k_scr = nc.dram_tensor("k_scr", [2, 2, 128, T], F32).ap()
    v_scr = nc.dram_tensor("v_scr", [2, 2, T, 128], F32).ap()
    rs_in = nc.dram_tensor("rs_in", [NT, D], F32).ap()
    rs_out = nc.dram_tensor("rs_out", [TPC, D], F32).ap()

    sems = {}
    for name in ("pe", "act", "dve", "cc"):
        sems[name] = st.enter_context(nc.semaphore(name))

    p = Prog(nc, sems)

    def sb(name, shape):
        return st.enter_context(nc.sbuf_tensor(name, list(shape), F32))

    iden = sb("iden_sb", (128, 128))
    masks = sb("masks_sb", (128, 4, 512))
    small = sb("small", (128, 32))
    gsc = sb("gsc", (128, 80))
    combine = sb("combine", (128, 4, EXP))

    pb = [st.enter_context(nc.psum_tensor(f"pb{i}", [128, 512], F32)) for i in range(8)]
    pbr = [Res(p) for _ in range(8)]

    def ser(eng, make, sem):
        p.wait(eng, "dve", p.cnt["dve"])
        p.wait(eng, "act", p.cnt["act"])
        return p.op(eng, make, sem=sem)

    r_iden, r_masks = Res(p), Res(p)
    r_eps = Res(p)
    ev0 = p.op("vector", lambda e: e.memset(small[:, 17:18], EPS), sem="dve")
    r_eps.setw("dve", ev0)
    p.dma("sync", "c_iden", iden[:, :], iden_in[:, :], writes=[r_iden])
    p.dma("sync", "c_masks", masks[:, :, :], masks_in[:, :, :], writes=[r_masks])

    r_q = [[Res(p) for _ in range(2)] for _ in range(2)]
    r_k = [[Res(p) for _ in range(2)] for _ in range(2)]
    r_v = [[Res(p) for _ in range(2)] for _ in range(2)]
    r_rsin = Res(p)
    r_rsout = Res(p)

    ph1 = ExitStack()
    wq_sb = ph1.enter_context(nc.sbuf_tensor("wq_sb", [128, NDC, 768], F32))
    xt_sb = ph1.enter_context(nc.sbuf_tensor("xt_sb", [128, 2, NDC, 512], F32))
    qk_tmp = ph1.enter_context(nc.sbuf_tensor("qk_tmp", [128, 4, 512], F32))
    v_tmp = ph1.enter_context(nc.sbuf_tensor("v_tmp", [128, 2, 4, 256], F32))

    r_wq = Res(p)
    r_xt = [Res(p), Res(p)]
    r_qk = [Res(p) for _ in range(4)]
    r_vt = [Res(p), Res(p)]

    p.dma("sync", "c_wq", wq_sb[:, :, :], qkvw.rearrange("(c p) n -> p c n", p=128), writes=[r_wq])

    NSL = NT // 512
    for sl in range(NSL):
        buf = sl % 2
        p.dma("sync", f"xt{buf}", xt_sb[:, buf, :, :],
              xfm[:, sl * 512:(sl + 1) * 512].rearrange("(c p) n -> p c n", p=128),
              writes=[r_xt[buf]])
        s, col = sl // 8, (sl % 8) * 512
        for unit in range(4):
            hh = unit % 2
            wcol = (unit // 2) * 256 + hh * 128
            bank = unit
            pbr[bank].write("tensor")
            r_wq.read("tensor")
            r_xt[buf].read("tensor")
            for dc in range(NDC):
                lhs = wq_sb[:, dc, wcol:wcol + 128]
                rhs = xt_sb[:, buf, dc, :]
                p.op("tensor",
                     lambda e, o=pb[bank][:, :], l=lhs, r=rhs, st_=(dc == 0), sp=(dc == NDC - 1):
                         e.matmul(o, l, r, start=st_, stop=sp),
                     sem="pe" if dc == NDC - 1 else None)
            v = p.cnt["pe"]
            pbr[bank].setw("pe", v)
            r_wq.setr("pe", v)
            r_xt[buf].setr("pe", v)
            pbr[bank].read("scalar")
            r_qk[unit].write("scalar")
            av = p.op("scalar", lambda e, o=qk_tmp[:, unit, :], i=pb[bank][:, :]:
                      e.activation(out=o, in_=i, func=AF.Copy), sem="act")
            pbr[bank].setr("act", av)
            r_qk[unit].setw("act", av)
            scr = (q_scr if unit < 2 else k_scr)[s, hh, :, col:col + 512]
            rres = (r_q if unit < 2 else r_k)[s][hh]
            r_qk[unit].read("gpsimd")
            gv = p.dma("gpsimd", f"qk{unit}", scr, qk_tmp[:, unit, :], writes=[rres])
            r_qk[unit].setr(*gv)
        vbuf = sl % 2
        r_vt[vbuf].write("scalar")
        for tt in range(4):
            bank = 4 + tt % 2
            pbr[bank].write("tensor")
            r_wq.read("tensor")
            r_xt[buf].read("tensor")
            for dc in range(NDC):
                lhs = xt_sb[:, buf, dc, tt * 128:(tt + 1) * 128]
                rhs = wq_sb[:, dc, 512:768]
                p.op("tensor",
                     lambda e, o=pb[bank][:, 0:256], l=lhs, r=rhs, st_=(dc == 0), sp=(dc == NDC - 1):
                         e.matmul(o, l, r, start=st_, stop=sp),
                     sem="pe" if dc == NDC - 1 else None)
            v = p.cnt["pe"]
            pbr[bank].setw("pe", v)
            r_wq.setr("pe", v)
            r_xt[buf].setr("pe", v)
            pbr[bank].read("scalar")
            av = p.op("scalar", lambda e, o=v_tmp[:, vbuf, tt, :], i=pb[bank][:, 0:256]:
                      e.activation(out=o, in_=i, func=AF.Copy), sem="act")
            pbr[bank].setr("act", av)
        r_vt[vbuf].setw("act", p.cnt["act"])
        for hh in range(2):
            r_vt[vbuf].read("gpsimd")
            gv = p.dma("gpsimd", f"vt{vbuf}{hh}",
                       v_scr[s, hh, col:col + 512, :].rearrange("(t p) n -> p t n", p=128),
                       v_tmp[:, vbuf, :, hh * 128:hh * 128 + 128],
                       writes=[r_v[s][hh]])
            r_vt[vbuf].setr(*gv)

    p.join_all()
    ph1.close()

    ph2 = ExitStack()
    K_sb = ph2.enter_context(nc.sbuf_tensor("K_sb", [128, T], F32))
    V_sb = ph2.enter_context(nc.sbuf_tensor("V_sb", [128, 32, 128], F32))
    Q_sb = ph2.enter_context(nc.sbuf_tensor("Q_sb", [128, T], F32))
    S_sb = ph2.enter_context(nc.sbuf_tensor("S_sb", [128, 2, T], F32))
    pt_sb = ph2.enter_context(nc.sbuf_tensor("pt_sb", [128, 2, 256], F32))
    A_sb = ph2.enter_context(nc.sbuf_tensor("A_sb", [128, 2, T], F32))
    wp_sb = ph2.enter_context(nc.sbuf_tensor("wp_sb", [128, 2, D], F32))
    ptmp = ph2.enter_context(nc.sbuf_tensor("ptmp", [128, 2, D], F32))

    rK, rV, rQ = Res(p), Res(p), Res(p)
    rS = [Res(p), Res(p)]
    rPT = [Res(p), Res(p)]
    rA = [Res(p), Res(p)]
    rWP = Res(p)
    rPtmp = [Res(p), Res(p)]
    rNM = [Res(p), Res(p)]
    rRD = [Res(p), Res(p)]

    p.dma("sync", "c_wp", wp_sb[:, :, :], wproj.rearrange("(h p) n -> p h n", p=128), writes=[rWP])

    NQT = T // 128

    for s in range(2):
        for hh in range(2):
            p.dma("sync", "sK", K_sb[:, :], k_scr[s, hh, :, :], reads=[r_k[s][hh]], writes=[rK])
            p.dma("sync", "sV", V_sb[:, :, :], v_scr[s, hh, :, :].rearrange("(c p) n -> p c n", p=128),
                  reads=[r_v[s][hh]], writes=[rV])
            p.dma("sync", "sQ", Q_sb[:, :], q_scr[s, hh, :, :], reads=[r_q[s][hh]], writes=[rQ])

            for j in range(NQT):
                nk = ceil_div(j + 1, 4)
                W = nk * 512
                slot = j % 2
                for kc in range(nk):
                    bank = (j * 8 + kc) % 2
                    pbr[bank].write("tensor")
                    rQ.read("tensor")
                    rK.read("tensor")
                    if kc == 0:
                        rS[slot].write("tensor")
                    lhs = Q_sb[:, j * 128:(j + 1) * 128]
                    rhs = K_sb[:, kc * 512:(kc + 1) * 512]
                    v = p.op("tensor", lambda e, o=pb[bank][:, :], l=lhs, r=rhs:
                             e.matmul(o, l, r, start=True, stop=True), sem="pe")
                    pbr[bank].setw("pe", v)
                    rQ.setr("pe", v)
                    rK.setr("pe", v)
                    pbr[bank].read("scalar")
                    rS[slot].write("scalar")
                    av = p.op("scalar", lambda e, o=S_sb[:, slot, kc * 512:(kc + 1) * 512],
                              i=pb[bank][:, :]: e.activation(out=o, in_=i, func=AF.Copy, scale=SCALE),
                              sem="act")
                    pbr[bank].setr("act", av)
                    rS[slot].setw("act", av)
                rS[slot].read("vector")
                r_masks.read("vector")
                dv = p.op("vector", lambda e, o=S_sb[:, slot, (j // 4) * 512:(j // 4) * 512 + 512],
                          m=masks[:, j % 4, :]: e.tensor_tensor(o, o, m, op=ALU.add), sem="dve")
                rS[slot].setw("dve", dv)
                rNM[slot].write("vector")
                rS[slot].read("vector")
                p.wait("vector", "dve", p.cnt["dve"])
                dv = ser("vector", lambda e, o=small[:, slot:slot + 1], i=S_sb[:, slot, 0:W]:
                          e.reduce_max(o, i, axis=AX.X, negate=True), sem="dve")
                rS[slot].setr("dve", dv)
                rNM[slot].setw("dve", dv)
                rS[slot].write("scalar")
                rNM[slot].read("scalar")
                av = p.op("scalar", lambda e, o=S_sb[:, slot, 0:W], b=small[:, slot:slot + 1],
                          a=small[:, 4 + slot:5 + slot]:
                          e.activation(out=o, in_=o, func=AF.Exp, bias=b, scale=1.0, accum_out=a),
                          sem="act")
                rS[slot].setw("act", av)
                rNM[slot].setr("act", av)
                rRD[slot].write("vector")
                dv = ser("vector", lambda e, o=small[:, 2 + slot:3 + slot],
                         i=small[:, 4 + slot:5 + slot]: e.reciprocal(o, i), sem="dve")
                rRD[slot].setw("dve", dv)
                rS[slot].write("vector")
                dv = ser("vector", lambda e, o=S_sb[:, slot, 0:W], r=small[:, 2 + slot:3 + slot]:
                         e.tensor_scalar_mul(o, o, r), sem="dve")
                rS[slot].setw("dve", dv)

                if j % 2 == 1:
                    g = j // 2
                    ng = 2 * (g + 1)
                    pvbank = 4
                    pbr[pvbank].write("tensor")
                    for tkc in range(ng):
                        tb = 2 + tkc % 2
                        pbr[tb].write("tensor")
                        rS[0].read("tensor")
                        rS[1].read("tensor")
                        r_iden.read("tensor")
                        p.op("tensor", lambda e, o=pb[tb][:, 0:128],
                             i=S_sb[:, 0, tkc * 128:(tkc + 1) * 128], d=iden[:, :]:
                             e.transpose(o, i, d))
                        v = p.op("tensor", lambda e, o=pb[tb][:, 128:256],
                                 i=S_sb[:, 1, tkc * 128:(tkc + 1) * 128], d=iden[:, :]:
                                 e.transpose(o, i, d), sem="pe")
                        pbr[tb].setw("pe", v)
                        rS[0].setr("pe", v)
                        rS[1].setr("pe", v)
                        ptb = tkc % 2
                        pbr[tb].read("vector")
                        rPT[ptb].write("vector")
                        dv = p.op("vector", lambda e, o=pt_sb[:, ptb, :], i=pb[tb][:, 0:256]:
                                  e.tensor_copy(o, i), sem="dve")
                        pbr[tb].setr("dve", dv)
                        rPT[ptb].setw("dve", dv)
                        rPT[ptb].read("tensor")
                        rV.read("tensor")
                        v = p.op("tensor", lambda e, o=pb[pvbank][:, 0:256], l=V_sb[:, tkc, :],
                                 r=pt_sb[:, ptb, :], st_=(tkc == 0), sp=(tkc == ng - 1):
                                 e.matmul(o, l, r, start=st_, stop=sp),
                                 sem="pe" if tkc == ng - 1 else None)
                        if v is not None:
                            pbr[pvbank].setw("pe", v)
                        rPT[ptb].setr("pe", p.cnt["pe"])
                        rV.setr("pe", p.cnt["pe"])
                    pbr[pvbank].read("scalar")
                    rA[hh].write("scalar")
                    av = p.op("scalar", lambda e, o=A_sb[:, hh, g * 256:(g + 1) * 256],
                              i=pb[pvbank][:, 0:256]: e.activation(out=o, in_=i, func=AF.Copy),
                              sem="act")
                    pbr[pvbank].setr("act", av)
                    rA[hh].setw("act", av)

        for tt in range(NQT):
            pbuf = tt % 2
            rPtmp[pbuf].write("scalar")
            for dsl in range(4):
                bank = 5 + (tt * 4 + dsl) % 2
                pbr[bank].write("tensor")
                rA[0].read("tensor")
                rA[1].read("tensor")
                rWP.read("tensor")
                for hh in range(2):
                    lhs = A_sb[:, hh, tt * 128:(tt + 1) * 128]
                    rhs = wp_sb[:, hh, dsl * 512:(dsl + 1) * 512]
                    p.op("tensor", lambda e, o=pb[bank][:, :], l=lhs, r=rhs, st_=(hh == 0),
                         sp=(hh == 1): e.matmul(o, l, r, start=st_, stop=sp),
                         sem="pe" if hh == 1 else None)
                v = p.cnt["pe"]
                pbr[bank].setw("pe", v)
                rA[0].setr("pe", v)
                rA[1].setr("pe", v)
                rWP.setr("pe", v)
                pbr[bank].read("scalar")
                av = p.op("scalar", lambda e, o=ptmp[:, pbuf, dsl * 512:(dsl + 1) * 512],
                          i=pb[bank][:, :]: e.activation(out=o, in_=i, func=AF.Copy), sem="act")
                pbr[bank].setr("act", av)
            rPtmp[pbuf].setw("act", p.cnt["act"])
            rPtmp[pbuf].read("gpsimd")
            gv = p.dma("gpsimd", f"pt{pbuf}", rs_in[s * T + tt * 128: s * T + (tt + 1) * 128, :],
                       ptmp[:, pbuf, :], writes=[r_rsin])
            rPtmp[pbuf].setr(*gv)

    p.join_all()
    ph2.close()

    r_rsin.read("gpsimd")
    r_rsout.write("gpsimd")
    cv = p.op("gpsimd", lambda e: e.collective_compute(
        "ReduceScatter", ALU.add, replica_groups=[list(range(NCORE))],
        ins=[rs_in[:, :]], outs=[rs_out[:, :]]), sem="cc")
    r_rsin.setr("cc", cv)
    r_rsout.setw("cc", cv)

    ph4 = ExitStack()
    x1fm = ph4.enter_context(nc.sbuf_tensor("x1fm", [128, NDC, 512], F32))
    y_sb = ph4.enter_context(nc.sbuf_tensor("y_sb", [128, 4, D], F32))
    g_sb = ph4.enter_context(nc.sbuf_tensor("g_sb", [128, NIC, 512], F32))
    w13_sb = ph4.enter_context(nc.sbuf_tensor("w13_sb", [128, 2, NDC, 256], F32))
    w2_sb = ph4.enter_context(nc.sbuf_tensor("w2_sb", [128, 2, NIC, 512], F32))
    xo_sb = ph4.enter_context(nc.sbuf_tensor("xo_sb", [128, D], F32))
    ro_sb = ph4.enter_context(nc.sbuf_tensor("ro_sb", [128, D], F32))
    gw_sb = ph4.enter_context(nc.sbuf_tensor("gw_sb", [128, NDC, EXP], F32))
    s1_sb = ph4.enter_context(nc.sbuf_tensor("s1_sb", [128, 512], F32))

    rX1, rY, rG = Res(p), Res(p), Res(p)
    rW13 = [Res(p), Res(p)]
    rW2 = [Res(p), Res(p)]
    rXO, rRO = Res(p), Res(p)
    rGW = Res(p)
    rS1 = Res(p)
    rSml = Res(p)
    rComb = Res(p)

    p.dma("sync", "c_gw", gw_sb[:, :, :], gatew.rearrange("(c p) n -> p c n", p=128), writes=[rGW])

    for pa in range(2):
        tok0 = pa * 512
        for tt in range(4):
            row = tok0 + tt * 128
            rXO.write("sync")
            p.dma("sync", "sxo", xo_sb[:, :], x_own[row:row + 128, :], writes=[rXO])
            rRO.write("sync")
            p.dma("sync", "sro", ro_sb[:, :], rs_out[row:row + 128, :], reads=[r_rsout], writes=[rRO])
            rXO.read("vector")
            rRO.read("vector")
            rY.write("vector")
            dv = p.op("vector", lambda e, o=y_sb[:, tt, :], a=xo_sb[:, :], b=ro_sb[:, :]:
                      e.tensor_add(o, a, b), sem="dve")
            rXO.setr("dve", dv)
            rRO.setr("dve", dv)
            rY.setw("dve", dv)
            rY.read("vector")
            rSml.write("vector")
            ser("vector", lambda e, o=small[:, 8:9], i=y_sb[:, tt, :]:
                 e.reduce_sum(o, i, axis=AX.X), sem="dve")
            dv = ser("vector", lambda e, o=small[:, 8:9]:
                      e.tensor_scalar_mul(o, o, 1.0 / D), sem="dve")
            rSml.setw("dve", dv)
            rY.read("scalar")
            rSml.read("scalar")
            rRO.write("scalar")
            av = ser("scalar", lambda e, o=ro_sb[:, :], i=y_sb[:, tt, :], a=small[:, 9:10]:
                      e.activation(out=o, in_=i, func=AF.Square, accum_out=a), sem="act")
            rRO.setw("act", av)
            rSml.setw("act", av)
            rY.setr("act", av)
            rSml.write("vector")
            p.wait("vector", "act", av)
            ser("vector", lambda e, o=small[:, 10:11], i=small[:, 9:10]:
                 e.tensor_scalar_mul(o, o if False else i, 1.0 / D), sem="dve")
            ser("vector", lambda e, o=small[:, 11:12], a=small[:, 8:9], b=small[:, 8:9]:
                 e.tensor_mul(o, a, b), sem="dve")
            dv = ser("vector", lambda e, o=small[:, 10:11], a=small[:, 10:11], b=small[:, 11:12]:
                      e.tensor_sub(o, a, b), sem="dve")
            rSml.setw("dve", dv)
            rSml.read("scalar")
            p.wait("scalar", "dve", dv)
            r_eps.read("scalar")
            av = ser("scalar", lambda e, o=small[:, 12:13], i=small[:, 10:11], b=small[:, 17:18]:
                      e.activation(out=o, in_=i, func=AF.Sqrt, bias=b, scale=1.0), sem="act")
            rSml.setw("act", av)
            p.wait("vector", "act", av)
            ser("vector", lambda e, o=small[:, 13:14], i=small[:, 12:13]:
                 e.reciprocal(o, i), sem="dve")
            ser("vector", lambda e, o=small[:, 14:15], a=small[:, 13:14], b=small[:, 13:14]:
                 e.tensor_mul(o, a, b), sem="dve")
            ser("vector", lambda e, o=small[:, 15:16], a=small[:, 10:11]:
                 e.tensor_scalar(o, a, EPS, None, ALU.add), sem="dve")
            ser("vector", lambda e, o=small[:, 14:15], a=small[:, 14:15], b=small[:, 15:16]:
                 e.tensor_mul(o, a, b), sem="dve")
            ser("vector", lambda e, o=small[:, 14:15], a=small[:, 14:15]:
                 e.tensor_scalar(o, a, -0.5, 1.5, ALU.mult, ALU.add), sem="dve")
            dv = ser("vector", lambda e, o=small[:, 13:14], a=small[:, 13:14], b=small[:, 14:15]:
                      e.tensor_mul(o, a, b), sem="dve")
            dv = ser("vector", lambda e, o=small[:, 16:17], a=small[:, 8:9], b=small[:, 13:14]:
                      e.scalar_tensor_tensor(o, a, -1.0, b, ALU.mult, ALU.mult), sem="dve")
            rSml.setw("dve", dv)
            rY.write("vector")
            rSml.read("vector")
            p.wait("vector", "dve", dv, force=True)
            av = ser("vector", lambda e, o=y_sb[:, tt, :], b=small[:, 16:17], s_=small[:, 13:14]:
                      e.tensor_scalar(o, o, s_, b, ALU.mult, ALU.add), sem="dve")
            rY.setw("dve", av)
            rSml.setr("dve", av)
            for dc in range(NDC):
                bank = 7 if dc % 2 == 0 else 3
                pbr[bank].write("tensor")
                rY.read("tensor")
                r_iden.read("tensor")
                v = p.op("tensor", lambda e, o=pb[bank][:, 0:128],
                         i=y_sb[:, tt, dc * 128:(dc + 1) * 128], d=iden[:, :]:
                         e.transpose(o, i, d), sem="pe")
                pbr[bank].setw("pe", v)
                rY.setr("pe", v)
                pbr[bank].read("vector")
                rX1.write("vector")
                dv = p.op("vector", lambda e, o=x1fm[:, dc, tt * 128:(tt + 1) * 128],
                          i=pb[bank][:, 0:128]: e.tensor_copy(o, i), sem="dve")
                pbr[bank].setr("dve", dv)
                rX1.setw("dve", dv)

        for tt in range(4):
            bank = 7
            pbr[bank].write("tensor")
            rX1.read("tensor")
            rGW.read("tensor")
            for dc in range(NDC):
                lhs = x1fm[:, dc, tt * 128:(tt + 1) * 128]
                rhs = gw_sb[:, dc, :]
                p.op("tensor", lambda e, o=pb[bank][:, 0:EXP], l=lhs, r=rhs, st_=(dc == 0),
                     sp=(dc == NDC - 1): e.matmul(o, l, r, start=st_, stop=sp),
                     sem="pe" if dc == NDC - 1 else None)
            v = p.cnt["pe"]
            pbr[bank].setw("pe", v)
            rX1.setr("pe", v)
            rGW.setr("pe", v)
            pbr[bank].read("scalar")
            rComb.write("scalar")
            av = p.op("scalar", lambda e, o=gsc[:, 0:16], i=pb[bank][:, 0:EXP]:
                      e.activation(out=o, in_=i, func=AF.Sigmoid), sem="act")
            pbr[bank].setr("act", av)
            rComb.setw("act", av)
            def gop(make):
                rComb.write("vector")
                dv = ser("vector", make, sem="dve")
                rComb.setw("dve", dv)
                return dv
            p.wait("vector", "act", av)
            gop(lambda e: e.reduce_max(gsc[:, 16:20],
                gsc[:, 0:16].rearrange("p (a b) -> p a b", a=4), axis=AX.X))
            gop(lambda e: e.reduce_max(gsc[:, 24:25], gsc[:, 16:20], axis=AX.X))
            dv = p.cnt["dve"]
            p.wait("vector", "dve", dv, force=True)
            gop(lambda e: e.tensor_scalar(gsc[:, 20:24], gsc[:, 16:20], gsc[:, 24:25], -2.0,
                                          ALU.is_ge, ALU.mult))
            gop(lambda e: e.tensor_add(gsc[:, 20:24], gsc[:, 20:24], gsc[:, 16:20]))
            gop(lambda e: e.reduce_max(gsc[:, 24:25], gsc[:, 20:24], axis=AX.X))
            dv = p.cnt["dve"]
            p.wait("vector", "dve", dv, force=True)
            gop(lambda e: e.tensor_scalar(gsc[:, 16:20], gsc[:, 16:20], gsc[:, 24:25], None,
                                          ALU.is_ge))
            gop(lambda e: e.tensor_tensor(gsc[:, 25:41].rearrange("p (a b) -> p a b", a=4),
                                          gsc[:, 0:16].rearrange("p (a b) -> p a b", a=4),
                                          gsc[:, 16:20].unsqueeze(2).broadcast_to([128, 4, 4]),
                                          op=ALU.mult))
            gop(lambda e: e.tensor_copy(gsc[:, 41:57], gsc[:, 25:41]))
            for _ in range(3):
                gop(lambda e: e.reduce_max(gsc[:, 24:25], gsc[:, 41:57], axis=AX.X))
                dv = p.cnt["dve"]
                p.wait("vector", "dve", dv, force=True)
                gop(lambda e: e.tensor_scalar(gsc[:, 57:73], gsc[:, 41:57], gsc[:, 24:25],
                                              -2.0, ALU.is_ge, ALU.mult))
                gop(lambda e: e.tensor_add(gsc[:, 41:57], gsc[:, 41:57], gsc[:, 57:73]))
            gop(lambda e: e.reduce_max(gsc[:, 24:25], gsc[:, 41:57], axis=AX.X))
            dv = p.cnt["dve"]
            p.wait("vector", "dve", dv, force=True)
            gop(lambda e: e.scalar_tensor_tensor(gsc[:, 41:57], gsc[:, 25:41], gsc[:, 24:25],
                                                 gsc[:, 25:41], ALU.is_ge, ALU.mult))
            gop(lambda e: e.reduce_sum(gsc[:, 73:74], gsc[:, 41:57], axis=AX.X))
            gop(lambda e: e.reciprocal(gsc[:, 74:75], gsc[:, 73:74]))
            dv = p.cnt["dve"]
            p.wait("vector", "dve", dv, force=True)
            rComb.write("vector")
            dv = ser("vector", lambda e, o=combine[:, tt, :], a=gsc[:, 41:57], r=gsc[:, 74:75]:
                      e.tensor_scalar_mul(o, a, r), sem="dve")
            rComb.setw("dve", dv)

        rCombRead = rComb

        for e_i in range(NE):
            for ic in range(NIC):
                wbuf = ic % 2
                rW13[wbuf].write("sync")
                p.dma("sync", f"w13{wbuf}", w13_sb[:, wbuf, :, :],
                      w13[e_i, :, ic, :].rearrange("(c p) n -> p c n", p=128),
                      writes=[rW13[wbuf]])
                b1, b3 = (0, 1) if ic % 2 == 0 else (2, 3)
                for bank, wc in ((b1, 0), (b3, 128)):
                    pbr[bank].write("tensor")
                    rW13[wbuf].read("tensor")
                    rX1.read("tensor")
                    for dc in range(NDC):
                        lhs = w13_sb[:, wbuf, dc, wc:wc + 128]
                        rhs = x1fm[:, dc, :]
                        p.op("tensor", lambda e, o=pb[bank][:, :], l=lhs, r=rhs, st_=(dc == 0),
                             sp=(dc == NDC - 1): e.matmul(o, l, r, start=st_, stop=sp),
                             sem="pe" if dc == NDC - 1 else None)
                    v = p.cnt["pe"]
                    pbr[bank].setw("pe", v)
                    rW13[wbuf].setr("pe", v)
                    rX1.setr("pe", v)
                pbr[b1].read("scalar")
                rS1.write("scalar")
                av = p.op("scalar", lambda e, o=s1_sb[:, :], i=pb[b1][:, :]:
                          e.activation(out=o, in_=i, func=AF.Silu), sem="act")
                pbr[b1].setr("act", av)
                rS1.setw("act", av)
                rS1.read("vector")
                pbr[b3].read("vector")
                rG.write("vector")
                dv = p.op("vector", lambda e, o=g_sb[:, ic, :], a=s1_sb[:, :], i=pb[b3][:, :]:
                          e.tensor_mul(o, a, i), sem="dve")
                rS1.setr("dve", dv)
                pbr[b3].setr("dve", dv)
                rG.setw("dve", dv)
            for dsl in range(4):
                wbuf = dsl % 2
                rW2[wbuf].write("sync")
                p.dma("sync", f"w2{wbuf}", w2_sb[:, wbuf, :, :],
                      w2[e_i, :, dsl * 512:(dsl + 1) * 512].rearrange("(c p) n -> p c n", p=128),
                      writes=[rW2[wbuf]])
                for tt in range(4):
                    bank = 4 + (dsl * 4 + tt) % 2
                    pbr[bank].write("tensor")
                    rW2[wbuf].read("tensor")
                    rG.read("tensor")
                    for ic in range(NIC):
                        lhs = g_sb[:, ic, tt * 128:(tt + 1) * 128]
                        rhs = w2_sb[:, wbuf, ic, :]
                        p.op("tensor", lambda e, o=pb[bank][:, :], l=lhs, r=rhs, st_=(ic == 0),
                             sp=(ic == NIC - 1): e.matmul(o, l, r, start=st_, stop=sp),
                             sem="pe" if ic == NIC - 1 else None)
                    v = p.cnt["pe"]
                    pbr[bank].setw("pe", v)
                    rW2[wbuf].setr("pe", v)
                    rG.setr("pe", v)
                    pbr[bank].read("vector")
                    rY.write("vector")
                    if e_i < EXP:
                        rCombRead.read("vector")
                        dv = p.op("vector", lambda e, o=y_sb[:, tt, dsl * 512:(dsl + 1) * 512],
                                  i=pb[bank][:, :], c=combine[:, tt, e_i:e_i + 1]:
                                  e.scalar_tensor_tensor(o, i, c, o, ALU.mult, ALU.add),
                                  sem="dve")
                        rCombRead.setr("dve", dv)
                    else:
                        dv = p.op("vector", lambda e, o=y_sb[:, tt, dsl * 512:(dsl + 1) * 512],
                                  i=pb[bank][:, :]: e.tensor_add(o, o, i), sem="dve")
                    pbr[bank].setr("dve", dv)
                    rY.setw("dve", dv)

        for tt in range(4):
            row = tok0 + tt * 128
            rY.read("vector")
            rSml.write("vector")
            ser("vector", lambda e, o=small[:, 8:9], i=y_sb[:, tt, :]:
                 e.reduce_sum(o, i, axis=AX.X), sem="dve")
            dv = ser("vector", lambda e, o=small[:, 8:9]:
                      e.tensor_scalar_mul(o, o, 1.0 / D), sem="dve")
            rSml.setw("dve", dv)
            rY.setr("dve", dv)
            rY.read("scalar")
            rSml.read("scalar")
            buf_r = rXO if tt % 2 == 0 else rRO
            buf_t = xo_sb if tt % 2 == 0 else ro_sb
            buf_r.write("scalar")
            av = ser("scalar", lambda e, o=buf_t[:, :], i=y_sb[:, tt, :], a=small[:, 9:10]:
                      e.activation(out=o, in_=i, func=AF.Square, accum_out=a), sem="act")
            buf_r.setw("act", av)
            rSml.setw("act", av)
            rY.setr("act", av)
            rSml.write("vector")
            p.wait("vector", "act", av)
            ser("vector", lambda e, o=small[:, 10:11], i=small[:, 9:10]:
                 e.tensor_scalar_mul(o, i, 1.0 / D), sem="dve")
            ser("vector", lambda e, o=small[:, 11:12], a=small[:, 8:9], b=small[:, 8:9]:
                 e.tensor_mul(o, a, b), sem="dve")
            dv = ser("vector", lambda e, o=small[:, 10:11], a=small[:, 10:11], b=small[:, 11:12]:
                      e.tensor_sub(o, a, b), sem="dve")
            rSml.setw("dve", dv)
            rSml.read("scalar")
            p.wait("scalar", "dve", dv)
            r_eps.read("scalar")
            av = ser("scalar", lambda e, o=small[:, 12:13], i=small[:, 10:11], b=small[:, 17:18]:
                      e.activation(out=o, in_=i, func=AF.Sqrt, bias=b, scale=1.0), sem="act")
            rSml.setw("act", av)
            p.wait("vector", "act", av)
            ser("vector", lambda e, o=small[:, 13:14], i=small[:, 12:13]:
                 e.reciprocal(o, i), sem="dve")
            ser("vector", lambda e, o=small[:, 14:15], a=small[:, 13:14], b=small[:, 13:14]:
                 e.tensor_mul(o, a, b), sem="dve")
            ser("vector", lambda e, o=small[:, 15:16], a=small[:, 10:11]:
                 e.tensor_scalar(o, a, EPS, None, ALU.add), sem="dve")
            ser("vector", lambda e, o=small[:, 14:15], a=small[:, 14:15], b=small[:, 15:16]:
                 e.tensor_mul(o, a, b), sem="dve")
            ser("vector", lambda e, o=small[:, 14:15], a=small[:, 14:15]:
                 e.tensor_scalar(o, a, -0.5, 1.5, ALU.mult, ALU.add), sem="dve")
            ser("vector", lambda e, o=small[:, 13:14], a=small[:, 13:14], b=small[:, 14:15]:
                 e.tensor_mul(o, a, b), sem="dve")
            dv = ser("vector", lambda e, o=small[:, 16:17], a=small[:, 8:9], b=small[:, 13:14]:
                      e.scalar_tensor_tensor(o, a, -1.0, b, ALU.mult, ALU.mult), sem="dve")
            rSml.setw("dve", dv)
            rY.read("vector")
            buf_r.write("vector")
            rSml.read("vector")
            p.wait("vector", "dve", dv, force=True)
            av = ser("vector", lambda e, o=buf_t[:, :], i=y_sb[:, tt, :], b=small[:, 16:17],
                      s_=small[:, 13:14]:
                      e.tensor_scalar(o, i, s_, b, ALU.mult, ALU.add), sem="dve")
            buf_r.setw("dve", av)
            rSml.setr("dve", av)
            rY.setr("dve", av)
            buf_r.read("gpsimd")
            gv = p.dma("gpsimd", f"sout{tt % 2}", out[row:row + 128, :], buf_t[:, :])
            buf_r.setr(*gv)

    for s_ in list(p.cnt):
        if s_.startswith(("sout", "qk", "vt", "pt")):
            p.wait("gpsimd", s_, p.cnt[s_], force=True)

    with nc.Block() as block:
        for eng in ENGS:
            getattr(block, eng)(lambda e, fns=p.fns[eng]: [f(e) for f in fns])

    ph4.close()
    st.close()
    return nc


_NC_CACHE = None


def _get_nc():
    global _NC_CACHE
    if _NC_CACHE is None:
        _NC_CACHE = build_nc()
    return _NC_CACHE


def make_in_maps(L0_attn_x, L0_attn_W_qkv, L0_attn_W_proj, L0_ffn_gate_w,
                 L0_ffn_W1, L0_ffn_W2, L0_ffn_W3, L0_ffn_Ws1, L0_ffn_Ws2, L0_ffn_Ws3):
    x = np.ascontiguousarray(np.asarray(L0_attn_x, dtype=np.float32)).reshape(NT, D)
    Wqkv = np.asarray(L0_attn_W_qkv, dtype=np.float32)
    Wproj = np.asarray(L0_attn_W_proj, dtype=np.float32)
    gate_w = np.asarray(L0_ffn_gate_w, dtype=np.float32)
    W1 = np.asarray(L0_ffn_W1, dtype=np.float32)
    W2 = np.asarray(L0_ffn_W2, dtype=np.float32)
    W3 = np.asarray(L0_ffn_W3, dtype=np.float32)
    Ws1 = np.asarray(L0_ffn_Ws1, dtype=np.float32)
    Ws2 = np.asarray(L0_ffn_Ws2, dtype=np.float32)
    Ws3 = np.asarray(L0_ffn_Ws3, dtype=np.float32)

    xfm = np.ascontiguousarray(x.T)
    WqkvT = np.ascontiguousarray(Wqkv.T)
    WprojT = np.ascontiguousarray(Wproj.T)
    gatewT = np.ascontiguousarray(gate_w.T)

    w13 = np.empty((NE, D, NIC, 256), dtype=np.float32)
    W1T = np.ascontiguousarray(W1.transpose(0, 2, 1))
    W3T = np.ascontiguousarray(W3.transpose(0, 2, 1))
    w13[:EXP, :, :, 0:128] = W1T.reshape(EXP, D, NIC, 128)
    w13[:EXP, :, :, 128:256] = W3T.reshape(EXP, D, NIC, 128)
    Ws1T = np.ascontiguousarray(Ws1.T).reshape(D, 2, NIC, 128)
    Ws3T = np.ascontiguousarray(Ws3.T).reshape(D, 2, NIC, 128)
    for half in range(2):
        w13[EXP + half, :, :, 0:128] = Ws1T[:, half]
        w13[EXP + half, :, :, 128:256] = Ws3T[:, half]

    w2all = np.empty((NE, IFF, D), dtype=np.float32)
    w2all[:EXP] = W2.transpose(0, 2, 1)
    Ws2T = np.ascontiguousarray(Ws2.T)
    w2all[EXP] = Ws2T[:IFF]
    w2all[EXP + 1] = Ws2T[IFF:]
    w2all = np.ascontiguousarray(w2all)

    iden = np.eye(128, dtype=np.float32)
    masks = np.zeros((128, 4, 512), dtype=np.float32)
    pidx = np.arange(128)[:, None]
    kidx = np.arange(512)[None, :]
    for j in range(4):
        masks[:, j, :] = np.where(kidx <= 128 * j + pidx, 0.0, NEGINF)

    in_maps = []
    for c in range(NCORE):
        h0 = 2 * c
        qkvw = np.ascontiguousarray(np.concatenate(
            [WqkvT[:, h0 * 128:(h0 + 2) * 128],
             WqkvT[:, D + h0 * 128:D + (h0 + 2) * 128],
             WqkvT[:, 2 * D + h0 * 128:2 * D + (h0 + 2) * 128]], axis=1))
        in_maps.append({
            "xfm": xfm,
            "x_own": np.ascontiguousarray(x[c * TPC:(c + 1) * TPC]),
            "qkvw": qkvw,
            "wproj": np.ascontiguousarray(WprojT[h0 * 128:(h0 + 2) * 128, :]),
            "gatew": gatewT,
            "w13": w13,
            "w2": w2all,
            "iden": iden,
            "masks": masks,
        })

    return in_maps


def kernel(**inputs):
    in_maps = make_in_maps(**inputs)
    nc = _get_nc()
    res = run_bass_kernel_spmd(nc, in_maps, core_ids=list(range(NCORE)))
    full = np.concatenate([res.results[c]["out"] for c in range(NCORE)], axis=0)
    return full.reshape(B, T, D).astype(np.float32, copy=False)
